# version 07
# baseline (speedup 1.0000x reference)
import numpy as np
from contextlib import ExitStack

import concourse.bass as bass
import concourse.mybir as mybir
from concourse.bass_utils import run_bass_kernel_spmd

F32 = mybir.dt.float32
BF16 = mybir.dt.bfloat16
AX = mybir.AxisListType.X
ALU = mybir.AluOpType
ACT = mybir.ActivationFunctionType

R = 8
T = 1024
C = 4096
CCH = 32
D = 1024
DCH = 8
F = 3072
FCH = 24
FG = 6
NG = FCH // FG
S = 8192
NTT = 8
EN = 256
ENC = C // EN
EPS = 1e-5
INV32 = 1.0 / 32.0
LADDER = [128.0, 64.0, 32.0, 16.0, 8.0, 4.0, 2.0, 1.0]


def build():
    nc = bass.Bass("TRN2", target_bir_lowering=False, debug=False)

    x_d = nc.dram_tensor("x", [T, C], F32, kind="ExternalInput").ap()
    wqkv_d = nc.dram_tensor("wqkv_s", [512, F], F32, kind="ExternalInput").ap()
    wo_d = nc.dram_tensor("wo_s", [128, C], F32, kind="ExternalInput").ap()
    w_d = nc.dram_tensor("w_s", [512, C], F32, kind="ExternalInput").ap()
    bqkv_d = nc.dram_tensor("bqkv", [F], F32, kind="ExternalInput").ap()
    bo_d = nc.dram_tensor("bo", [C], F32, kind="ExternalInput").ap()
    eye_d = nc.dram_tensor("eye", [128, 128], F32, kind="ExternalInput").ap()
    eyeb_d = nc.dram_tensor("eyeb", [128, 128], BF16, kind="ExternalInput").ap()
    out_d = nc.dram_tensor("out", [T, C], F32, kind="ExternalOutput").ap()

    sp_xnh = nc.dram_tensor("sp_xnh", [CCH, 128, T], BF16).ap()
    sp_xnl = nc.dram_tensor("sp_xnl", [CCH, 128, T], BF16).ap()
    sp_x8 = nc.dram_tensor("sp_x8", [CCH, 128, T], BF16).ap()
    sp_qh = nc.dram_tensor("sp_qh", [DCH, 128, T], BF16).ap()
    sp_ql = nc.dram_tensor("sp_ql", [DCH, 128, T], BF16).ap()
    sp_oh = nc.dram_tensor("sp_oh", [DCH, 128, T], BF16).ap()
    sp_ol = nc.dram_tensor("sp_ol", [DCH, 128, T], BF16).ap()
    sp_m = nc.dram_tensor("sp_m", [T], F32).ap()

    agi_wqh = nc.dram_tensor("agi_wqh", [512, F], BF16).ap()
    agi_wql = nc.dram_tensor("agi_wql", [512, F], BF16).ap()
    ago_wqh = nc.dram_tensor("ago_wqh", [C, F], BF16, addr_space="Shared").ap()
    ago_wql = nc.dram_tensor("ago_wql", [C, F], BF16, addr_space="Shared").ap()
    agi_woh = nc.dram_tensor("agi_woh", [128, C], BF16).ap()
    agi_wol = nc.dram_tensor("agi_wol", [128, C], BF16).ap()
    ago_woh = nc.dram_tensor("ago_woh", [D, C], BF16, addr_space="Shared").ap()
    ago_wol = nc.dram_tensor("ago_wol", [D, C], BF16, addr_space="Shared").ap()
    agi_w8t = nc.dram_tensor("agi_w8t", [CCH, 128, 512], BF16).ap()
    ago_w8t = nc.dram_tensor("ago_w8t", [R, CCH, 128, 512], BF16, addr_space="Shared").ap()
    agi_wsc = nc.dram_tensor("agi_wsc", [512], F32).ap()
    ago_wsc = nc.dram_tensor("ago_wsc", [R * 512], F32, addr_space="Shared").ap()
    agi_kh = nc.dram_tensor("agi_kh", [DCH, 128, T], BF16).ap()
    agi_kl = nc.dram_tensor("agi_kl", [DCH, 128, T], BF16).ap()
    ago_kh = nc.dram_tensor("ago_kh", [R, DCH, 128, T], BF16, addr_space="Shared").ap()
    ago_kl = nc.dram_tensor("ago_kl", [R, DCH, 128, T], BF16, addr_space="Shared").ap()
    agi_vh = nc.dram_tensor("agi_vh", [NTT, 128, D], BF16).ap()
    agi_vl = nc.dram_tensor("agi_vl", [NTT, 128, D], BF16).ap()
    ago_vh = nc.dram_tensor("ago_vh", [R, NTT, 128, D], BF16, addr_space="Shared").ap()
    ago_vl = nc.dram_tensor("ago_vl", [R, NTT, 128, D], BF16, addr_space="Shared").ap()

    with ExitStack() as ctx:
        sem = {}
        for nm in ["LD", "GD", "SD", "CC", "tpa", "vpa",
                   "vaslot", "vawq", "vstat", "vmcol", "vasp", "sse", "tbias",
                   "tbw", "vbe", "vbq", "vbx",
                   "tds", "vds", "vdm", "ssx", "vde", "tdp", "vdo",
                   "tef", "vgs", "ssl", "vee"]:
            sem[nm] = ctx.enter_context(nc.semaphore(nm))

        eye = ctx.enter_context(nc.sbuf_tensor("eye_s", [128, 128], F32))
        eyeb = ctx.enter_context(nc.sbuf_tensor("eyeb_s", [128, 128], BF16))
        bq_cols = ctx.enter_context(nc.sbuf_tensor("bq_cols", [128, FCH], F32))
        xs_all = ctx.enter_context(nc.sbuf_tensor("xs_all", [128, NTT], F32))
        m_all = ctx.enter_context(nc.sbuf_tensor("m_all", [128, NTT], F32))
        st_t = ctx.enter_context(nc.sbuf_tensor("st_t", [128, 16], F32))
        wsc_cols = ctx.enter_context(nc.sbuf_tensor("wsc_cols", [128, 4], F32))

        pA = [ctx.enter_context(nc.psum_tensor(f"pA{i}", [128, 1024], BF16))
              for i in range(2)]
        pB = [ctx.enter_context(nc.psum_tensor(f"pB{i}", [128, 512], F32))
              for i in range(6)]

        block = ctx.enter_context(nc.Block())

        pa_t = {"n": 0}
        pa_v = {"n": 0}

        def pa_group(tensor, pieces):
            k = pa_t["n"]
            if k >= 2:
                tensor.wait_ge(sem["vpa"], k - 1)
            bank = pA[k % 2]
            t_ = None
            for u_, src in enumerate(pieces):
                t_ = tensor.transpose(bank[:, u_ * 128:(u_ + 1) * 128], src, eyeb[:, :])
            t_.then_inc(sem["tpa"], 1)
            pa_t["n"] += 1
            return k

        def pa_evac(vector, dst, extra_wait=None, width=1024):
            k = pa_v["n"]
            vector.wait_ge(sem["tpa"], k + 1)
            if extra_wait is not None:
                extra_wait()
            vector.tensor_copy(out=dst, in_=pA[k % 2][:, 0:width])
            vector.tensor_copy(out=st_t[:, 12:13], in_=st_t[:, 13:14]
                               ).then_inc(sem["vpa"], 1)
            pa_v["n"] += 1
            return k

        actx = ExitStack()
        with actx:
            xt = actx.enter_context(nc.sbuf_tensor("A_xt", [128, 2, C], F32))
            xn = actx.enter_context(nc.sbuf_tensor("A_xn", [128, C], F32))
            sq = actx.enter_context(nc.sbuf_tensor("A_sq", [128, C], F32))
            xh_t = actx.enter_context(nc.sbuf_tensor("A_xh", [128, C], BF16))
            xl_t = actx.enter_context(nc.sbuf_tensor("A_xl", [128, C], BF16))
            tstage = actx.enter_context(nc.sbuf_tensor("A_ts", [128, 2, 2, C], BF16))
            wsp_h = actx.enter_context(nc.sbuf_tensor("A_wh", [128, C], BF16))
            wsp_l = actx.enter_context(nc.sbuf_tensor("A_wl", [128, C], BF16))
            bqr = actx.enter_context(nc.sbuf_tensor("A_bqr", [1, F], F32))

            @block.sync
            def _(sync):
                sync.dma_start(out=eye[:], in_=eye_d[:, :]).then_inc(sem["LD"], 16)
                sync.dma_start(out=eyeb[:], in_=eyeb_d[:, :]).then_inc(sem["LD"], 16)
                sync.dma_start(out=bqr[:], in_=bqkv_d[:].unsqueeze(0)
                               ).then_inc(sem["LD"], 16)
                for j in range(13):
                    if j >= 2:
                        sync.wait_ge(sem["vaslot"], j - 1)
                    if j < 4:
                        src, dst = wqkv_d[j * 128:(j + 1) * 128, :], xt[:, j % 2, 0:F]
                    elif j == 4:
                        src, dst = wo_d[:, :], xt[:, j % 2]
                    else:
                        tt = j - 5
                        src, dst = x_d[tt * 128:(tt + 1) * 128, :], xt[:, j % 2]
                    sync.dma_start(out=dst, in_=src).then_inc(sem["LD"], 16)

            LD_A = 16 * 16

            @block.gpsimd
            def _(gpsimd):
                for j in range(4):
                    gpsimd.wait_ge(sem["vawq"], j + 1)
                    gpsimd.dma_start(out=agi_wqh[j * 128:(j + 1) * 128, :],
                                     in_=wsp_h[:, 0:F]).then_inc(sem["GD"], 16)
                    gpsimd.dma_start(out=agi_wql[j * 128:(j + 1) * 128, :],
                                     in_=wsp_l[:, 0:F]).then_inc(sem["GD"], 16)
                gpsimd.wait_ge(sem["vawq"], 5)
                gpsimd.dma_start(out=agi_woh[:, :], in_=wsp_h[:]).then_inc(sem["GD"], 16)
                gpsimd.dma_start(out=agi_wol[:, :], in_=wsp_l[:]).then_inc(sem["GD"], 16)
                gpsimd.wait_ge(sem["GD"], 160)
                for ci, (nm_i, nm_o) in enumerate(((agi_wqh, ago_wqh), (agi_wql, ago_wql),
                                                   (agi_woh, ago_woh), (agi_wol, ago_wol))):
                    if ci > 0:
                        gpsimd.wait_ge(sem["CC"], ci)
                    gpsimd.collective_compute(
                        "AllGather", ALU.bypass, replica_groups=[list(range(R))],
                        ins=[nm_i.opt()], outs=[nm_o.opt()]).then_inc(sem["CC"], 1)
                for tt in range(NTT):
                    gpsimd.wait_ge(sem["vpa"], tt * 8 + 8)
                    for grp in range(4):
                        gpsimd.dma_start(
                            out=sp_xnl[grp * 8:(grp + 1) * 8, :,
                                       tt * 128:(tt + 1) * 128].transpose([1, 0, 2]),
                            in_=tstage[:, tt % 2, 1, grp * 1024:(grp + 1) * 1024]
                               .rearrange("p (cc q) -> p cc q", cc=8),
                        ).then_inc(sem["GD"], 16)

            GD_A = 160 + 32 * 16

            @block.scalar
            def _(scalar):
                for tt in range(NTT):
                    scalar.wait_ge(sem["vstat"], tt + 1)
                    scalar.activation(out=st_t[:, 4:5], in_=st_t[:, 3:4], func=ACT.Ln)
                    scalar.copy(out=st_t[:, 14:15], in_=st_t[:, 15:16])
                    scalar.activation(out=st_t[:, 5:6], in_=st_t[:, 4:5], func=ACT.Exp,
                                      scale=-0.5)
                    scalar.copy(out=st_t[:, 14:15], in_=st_t[:, 15:16]
                                ).then_inc(sem["sse"], 1)
                    scalar.wait_ge(sem["vmcol"], tt + 1)
                    scalar.dma_start(out=sp_m[tt * 128:(tt + 1) * 128].unsqueeze(1),
                                     in_=m_all[:, tt:tt + 1]).then_inc(sem["SD"], 16)
                    scalar.wait_ge(sem["vpa"], tt * 8 + 4)
                    for grp in range(4):
                        scalar.dma_start(
                            out=sp_xnh[grp * 8:(grp + 1) * 8, :,
                                       tt * 128:(tt + 1) * 128].transpose([1, 0, 2]),
                            in_=tstage[:, tt % 2, 0, grp * 1024:(grp + 1) * 1024]
                               .rearrange("p (cc q) -> p cc q", cc=8),
                        ).then_inc(sem["SD"], 16)

            SD_A = 8 * 5 * 16

            @block.tensor
            def _(tensor):
                tensor.wait_ge(sem["LD"], 48)
                t_ = None
                for fc in range(FCH):
                    t_ = tensor.transpose(pB[0][:, fc:fc + 1],
                                          bqr[0:1, fc * 128:(fc + 1) * 128],
                                          eye[0:1, 0:1])
                t_.then_inc(sem["tbias"], 1)
                for tt in range(NTT):
                    tensor.wait_ge(sem["vasp"], tt + 1)
                    for src in (xh_t, xl_t):
                        for grp in range(4):
                            pieces = [src[:, (grp * 8 + u_) * 128:(grp * 8 + u_ + 1) * 128]
                                      for u_ in range(8)]
                            pa_group(tensor, pieces)

            @block.vector
            def _(vector):
                vector.memset(st_t[:], 0.0)
                for j in range(5):
                    vector.wait_ge(sem["LD"], 48 + 16 * (j + 1))
                    if j > 0:
                        vector.wait_ge(sem["GD"], 32 * j)
                    wdt = F if j < 4 else C
                    vector.tensor_copy(out=wsp_h[:, 0:wdt], in_=xt[:, j % 2, 0:wdt])
                    vector.tensor_tensor(out=wsp_l[:, 0:wdt], in0=xt[:, j % 2, 0:wdt],
                                         in1=wsp_h[:, 0:wdt], op=ALU.subtract
                                         ).then_inc(sem["vawq"], 1)
                    vector.tensor_copy(out=st_t[:, 12:13], in_=st_t[:, 13:14]
                                       ).then_inc(sem["vaslot"], 1)
                vector.wait_ge(sem["tbias"], 1)
                vector.tensor_copy(out=bq_cols[:], in_=pB[0][:, 0:FCH])
                for tt in range(NTT):
                    j = 5 + tt
                    vector.wait_ge(sem["LD"], 48 + 16 * (j + 1))
                    vector.tensor_reduce(out=st_t[:, 0:1], in_=xt[:, j % 2], axis=AX,
                                         op=ALU.add)
                    vector.tensor_copy(out=st_t[:, 12:13], in_=st_t[:, 13:14])
                    vector.tensor_scalar(out=st_t[:, 1:2], in0=st_t[:, 0:1],
                                         scalar1=1.0 / C, scalar2=None, op0=ALU.mult)
                    vector.tensor_copy(out=st_t[:, 12:13], in_=st_t[:, 13:14])
                    vector.tensor_scalar(out=xn[:], in0=xt[:, j % 2],
                                         scalar1=st_t[:, 1:2], scalar2=None,
                                         op0=ALU.subtract).then_inc(sem["vaslot"], 1)
                    vector.tensor_tensor(out=sq[:], in0=xn[:], in1=xn[:], op=ALU.mult)
                    vector.tensor_reduce(out=st_t[:, 2:3], in_=sq[:], axis=AX, op=ALU.add)
                    vector.tensor_copy(out=st_t[:, 12:13], in_=st_t[:, 13:14])
                    vector.tensor_scalar(out=st_t[:, 3:4], in0=st_t[:, 2:3],
                                         scalar1=1.0 / C, scalar2=EPS,
                                         op0=ALU.mult, op1=ALU.add)
                    vector.tensor_copy(out=st_t[:, 12:13], in_=st_t[:, 13:14]
                                       ).then_inc(sem["vstat"], 1)
                    vector.wait_ge(sem["sse"], tt + 1)
                    vector.tensor_scalar(out=xn[:], in0=xn[:], scalar1=st_t[:, 5:6],
                                         scalar2=None, op0=ALU.mult)
                    vector.tensor_reduce(out=st_t[:, 6:7], in_=xn[:], axis=AX,
                                         op=ALU.max, apply_absolute_value=True)
                    vector.tensor_copy(out=st_t[:, 12:13], in_=st_t[:, 13:14])
                    vector.reciprocal(out=st_t[:, 7:8], in_=st_t[:, 6:7])
                    vector.tensor_copy(out=st_t[:, 12:13], in_=st_t[:, 13:14])
                    vector.tensor_scalar(out=m_all[:, tt:tt + 1], in0=st_t[:, 7:8],
                                         scalar1=127.0, scalar2=None, op0=ALU.mult)
                    vector.tensor_copy(out=st_t[:, 12:13], in_=st_t[:, 13:14]
                                       ).then_inc(sem["vmcol"], 1)
                    vector.tensor_scalar(out=xs_all[:, tt:tt + 1], in0=st_t[:, 6:7],
                                         scalar1=1.0 / 127.0, scalar2=None, op0=ALU.mult)
                    vector.tensor_copy(out=xh_t[:], in_=xn[:])
                    vector.tensor_tensor(out=xl_t[:], in0=xn[:], in1=xh_t[:],
                                         op=ALU.subtract).then_inc(sem["vasp"], 1)
                    for gi in range(8):
                        hidx, grp = gi // 4, gi % 4
                        extra = None
                        if tt >= 2 and gi == 0:
                            def extra(vector=vector, tt=tt):
                                vector.wait_ge(sem["SD"], 16 * 5 * (tt - 1))
                                vector.wait_ge(sem["GD"], 160 + 16 * 4 * (tt - 1))
                        pa_evac(vector,
                                tstage[:, tt % 2, hidx, grp * 1024:(grp + 1) * 1024],
                                extra_wait=extra)

        PA_A = 64
        assert pa_t["n"] == PA_A and pa_v["n"] == PA_A

        bctx = ExitStack()
        with bctx:
            xnfh = bctx.enter_context(nc.sbuf_tensor("B_xh", [128, CCH, 512], BF16))
            xnfl = bctx.enter_context(nc.sbuf_tensor("B_xl", [128, CCH, 512], BF16))
            m_b = bctx.enter_context(nc.sbuf_tensor("B_mb", [128, T], F32))
            wqh_b = bctx.enter_context(nc.sbuf_tensor("B_wqh", [128, 2, FG * 128], BF16))
            wql_b = bctx.enter_context(nc.sbuf_tensor("B_wql", [128, 2, FG * 128], BF16))
            qstage = bctx.enter_context(nc.sbuf_tensor("B_qs", [128, DCH, 2, 512], BF16))
            kstage = bctx.enter_context(nc.sbuf_tensor("B_ks", [128, DCH, 2, 512], BF16))
            vsplit = bctx.enter_context(nc.sbuf_tensor("B_vs", [128, DCH, 2, 512], BF16))
            fstg = bctx.enter_context(nc.sbuf_tensor("B_fs", [128, 512], F32))
            wq_in = bctx.enter_context(nc.sbuf_tensor("B_wi", [128, C], F32))
            u0b = bctx.enter_context(nc.sbuf_tensor("B_u0", [128, 1024], F32))
            ub = bctx.enter_context(nc.sbuf_tensor("B_u", [128, 1024], F32))
            dbu = bctx.enter_context(nc.sbuf_tensor("B_d", [128, 1024], F32))
            w8c = bctx.enter_context(nc.sbuf_tensor("B_w8", [128, 2, C], BF16))
            w8ts = bctx.enter_context(nc.sbuf_tensor("B_w8t", [128, 2, 1024], BF16))
            x8st = bctx.enter_context(nc.sbuf_tensor("B_x8", [128, 2, 512], BF16))
            vtst = bctx.enter_context(nc.sbuf_tensor("B_vt", [128, 2, 512], BF16))

            sd_v = SD_A
            sd_wc_done = {}
            sd_q_done = {}
            sd_x8_done = {}
            for h in range(2):
                for jj in range(2):
                    sd_v += 16
                    sd_wc_done[2 * h + jj] = sd_v
                for g in range(NG):
                    for fc in range(g * FG, (g + 1) * FG):
                        if fc < 8:
                            sd_v += 32
                            sd_q_done[(h, fc)] = sd_v
                    for u_ in range(8):
                        sd_v += 16
                        sd_x8_done[(h, g * 8 + u_)] = sd_v
            SD_B = sd_v

            gd_v = GD_A
            gd_wql_done = {}
            gd_k_done = {}
            gd_w8_done = {}
            gd_v_done = {}
            for h in range(2):
                for i in range(NG * CCH):
                    gd_v += 16
                    gd_wql_done[(h, i)] = gd_v
                for dc in range(DCH):
                    gd_v += 32
                    gd_k_done[(h, dc)] = gd_v
                for jj in range(2):
                    for grp in range(4):
                        gd_v += 16
                        gd_w8_done[(2 * h + jj, grp)] = gd_v
                for dc in range(DCH):
                    for hl in range(2):
                        gd_v += 16
                        gd_v_done[(h, dc, hl)] = gd_v
            gd_v += 4 * 16
            GD_B = gd_v

            pa_w8 = {}
            pa_vt = {}
            k0 = PA_A
            for h in range(2):
                for jj in range(2):
                    for grp in range(4):
                        pa_w8[(2 * h + jj, grp)] = k0
                        k0 += 1
                for dc in range(DCH):
                    for hl in range(2):
                        pa_vt[(h, dc, hl)] = k0
                        k0 += 1
            PA_B = k0

            def ld_xnf_done(h):
                return LD_A + 16 * (1 + h * 192 + 64)

            def ld_wq_done(h, i):
                return LD_A + 16 * (1 + h * 192 + 64 + (i + 1))

            LD_B = LD_A + 16 * (1 + 2 * 192)

            @block.sync
            def _(sync):
                sync.wait_ge(sem["vpa"], PA_A)
                sync.wait_ge(sem["SD"], SD_A)
                sync.wait_ge(sem["GD"], GD_A)
                sync.wait_ge(sem["CC"], 2)
                sync.dma_start(out=m_b[:], in_=sp_m[:].unsqueeze(0)
                               .broadcast_to([128, T])).then_inc(sem["LD"], 16)
                for h in range(2):
                    for cc in range(CCH):
                        if h == 1 and cc == 0:
                            sync.wait_ge(sem["tbw"], 128)
                            sync.wait_ge(sem["vbx"], 32)
                        sync.dma_start(out=xnfh[:, cc],
                                       in_=sp_xnh[cc, :, h * 512:(h + 1) * 512]
                                       ).then_inc(sem["LD"], 16)
                        sync.dma_start(out=xnfl[:, cc],
                                       in_=sp_xnl[cc, :, h * 512:(h + 1) * 512]
                                       ).then_inc(sem["LD"], 16)
                    for i in range(NG * CCH):
                        g, c = divmod(i, CCH)
                        if h * 128 + i >= 2:
                            sync.wait_ge(sem["tbw"], h * 128 + i - 1)
                        sync.dma_start(
                            out=wqh_b[:, i % 2],
                            in_=ago_wqh[c * 128:(c + 1) * 128,
                                        g * FG * 128:(g + 1) * FG * 128]
                        ).then_inc(sem["LD"], 16)

            @block.gpsimd
            def _(gpsimd):
                gpsimd.wait_ge(sem["vpa"], PA_A)
                for h in range(2):
                    for i in range(NG * CCH):
                        if h * 128 + i >= 2:
                            gpsimd.wait_ge(sem["tbw"], h * 128 + i - 1)
                        g, c = divmod(i, CCH)
                        gpsimd.dma_start(
                            out=wql_b[:, i % 2],
                            in_=ago_wql[c * 128:(c + 1) * 128,
                                        g * FG * 128:(g + 1) * FG * 128]
                        ).then_inc(sem["GD"], 16)
                    for dc in range(DCH):
                        fc = 8 + dc
                        gpsimd.wait_ge(sem["vbe"], h * FCH + fc + 1)
                        gpsimd.dma_start(out=agi_kh[dc, :, h * 512:(h + 1) * 512],
                                         in_=kstage[:, dc, 0]).then_inc(sem["GD"], 16)
                        gpsimd.dma_start(out=agi_kl[dc, :, h * 512:(h + 1) * 512],
                                         in_=kstage[:, dc, 1]).then_inc(sem["GD"], 16)
                    for jj in range(2):
                        j = 2 * h + jj
                        for grp in range(4):
                            gpsimd.wait_ge(sem["vpa"], pa_w8[(j, grp)] + 1)
                            gpsimd.dma_start(
                                out=agi_w8t[grp * 8:(grp + 1) * 8, :,
                                            j * 128:(j + 1) * 128].transpose([1, 0, 2]),
                                in_=w8ts[:, pa_w8[(j, grp)] % 2]
                                   .rearrange("p (cc q) -> p cc q", cc=8),
                            ).then_inc(sem["GD"], 16)
                    for dc in range(DCH):
                        for hl in range(2):
                            gpsimd.wait_ge(sem["vpa"], pa_vt[(h, dc, hl)] + 1)
                            dst = agi_vh if hl == 0 else agi_vl
                            gpsimd.dma_start(
                                out=dst[h * 4:(h + 1) * 4, :,
                                        dc * 128:(dc + 1) * 128].transpose([1, 0, 2]),
                                in_=vtst[:, pa_vt[(h, dc, hl)] % 2]
                                   .rearrange("p (ts q) -> p ts q", ts=4),
                            ).then_inc(sem["GD"], 16)
                gpsimd.wait_ge(sem["vbq"], 4)
                for j in range(4):
                    gpsimd.dma_start(out=agi_wsc[j * 128:(j + 1) * 128].unsqueeze(1),
                                     in_=wsc_cols[:, j:j + 1]).then_inc(sem["GD"], 16)
                gates = [gd_k_done[(1, DCH - 1)], gd_k_done[(1, DCH - 1)],
                         gd_v_done[(1, DCH - 1, 1)], gd_v_done[(1, DCH - 1, 1)],
                         gd_w8_done[(3, 3)], GD_B]
                for ci, (nm_i, nm_o) in enumerate(((agi_kh, ago_kh), (agi_kl, ago_kl),
                                                   (agi_vh, ago_vh), (agi_vl, ago_vl),
                                                   (agi_w8t, ago_w8t), (agi_wsc, ago_wsc))):
                    gpsimd.wait_ge(sem["GD"], gates[ci])
                    gpsimd.wait_ge(sem["CC"], 4 + ci)
                    gpsimd.collective_compute(
                        "AllGather", ALU.bypass, replica_groups=[list(range(R))],
                        ins=[nm_i.opt()], outs=[nm_o.opt()]).then_inc(sem["CC"], 1)

            @block.scalar
            def _(scalar):
                scalar.wait_ge(sem["vpa"], PA_A)
                for h in range(2):
                    for jj in range(2):
                        j = 2 * h + jj
                        scalar.wait_ge(sem["vbq"], j)
                        scalar.dma_start(out=wq_in[:],
                                         in_=w_d[j * 128:(j + 1) * 128, :]
                                         ).then_inc(sem["SD"], 16)
                    for g in range(NG):
                        for fc in range(g * FG, (g + 1) * FG):
                            if fc < 8:
                                scalar.wait_ge(sem["vbe"], h * FCH + fc + 1)
                                scalar.dma_start(
                                    out=sp_qh[fc, :, h * 512:(h + 1) * 512],
                                    in_=qstage[:, fc, 0]).then_inc(sem["SD"], 16)
                                scalar.dma_start(
                                    out=sp_ql[fc, :, h * 512:(h + 1) * 512],
                                    in_=qstage[:, fc, 1]).then_inc(sem["SD"], 16)
                        for u_ in range(8):
                            pc = g * 8 + u_
                            scalar.wait_ge(sem["vbx"], h * 32 + pc + 1)
                            scalar.dma_start(
                                out=sp_x8[pc, :, h * 512:(h + 1) * 512],
                                in_=x8st[:, pc % 2]).then_inc(sem["SD"], 16)

            @block.tensor
            def _(tensor):
                tensor.wait_ge(sem["vpa"], PA_A)
                for h in range(2):
                    for i in range(NG * CCH):
                        g, c = divmod(i, CCH)
                        gi = h * 128 + i
                        tensor.wait_ge(sem["LD"], max(ld_wq_done(h, i), ld_xnf_done(h)))
                        tensor.wait_ge(sem["GD"], gd_wql_done[(h, i)])
                        if c == 0 and gi > 0:
                            tensor.wait_ge(sem["vbe"], h * FCH + g * FG)
                        b = i % 2
                        last = None
                        for f in range(FG):
                            wh_t = wqh_b[:, b, f * 128:(f + 1) * 128]
                            wl_t = wql_b[:, b, f * 128:(f + 1) * 128]
                            xh_ = xnfh[:, c]
                            xl_ = xnfl[:, c]
                            tensor.matmul(pB[f][:], wh_t, xh_, start=(c == 0), stop=False)
                            tensor.matmul(pB[f][:], wh_t, xl_, start=False, stop=False)
                            last = tensor.matmul(pB[f][:], wl_t, xh_, start=False,
                                                 stop=(c == CCH - 1))
                        last.then_inc(sem["tbw"], 1)
                    for jj in range(2):
                        j = 2 * h + jj
                        tensor.wait_ge(sem["vbq"], j + 1)
                        for grp in range(4):
                            pieces = [w8c[:, j % 2, (grp * 8 + u_) * 128:
                                          (grp * 8 + u_ + 1) * 128] for u_ in range(8)]
                            kk = pa_group(tensor, pieces)
                            assert kk == pa_w8[(j, grp)]
                    tensor.wait_ge(sem["vbe"], (h + 1) * FCH)
                    for dc in range(DCH):
                        for hl in range(2):
                            pieces = [vsplit[:, dc, hl, u_ * 128:(u_ + 1) * 128]
                                      for u_ in range(4)]
                            kk = pa_group(tensor, pieces)
                            assert kk == pa_vt[(h, dc, hl)]

            @block.vector
            def _(vector):
                def wquant(j):
                    vector.wait_ge(sem["SD"], sd_wc_done[j])
                    vector.tensor_reduce(out=st_t[:, 8:9], in_=wq_in[:], axis=AX,
                                         op=ALU.max, apply_absolute_value=True)
                    vector.tensor_copy(out=st_t[:, 12:13], in_=st_t[:, 13:14])
                    vector.reciprocal(out=st_t[:, 9:10], in_=st_t[:, 8:9])
                    vector.tensor_copy(out=st_t[:, 12:13], in_=st_t[:, 13:14])
                    vector.tensor_scalar(out=st_t[:, 10:11], in0=st_t[:, 9:10],
                                         scalar1=127.0, scalar2=None, op0=ALU.mult)
                    vector.tensor_scalar(out=wsc_cols[:, j:j + 1], in0=st_t[:, 8:9],
                                         scalar1=1.0 / 127.0, scalar2=None, op0=ALU.mult)
                    vector.tensor_copy(out=st_t[:, 12:13], in_=st_t[:, 13:14])

                    if j >= 2:
                        vector.wait_ge(sem["tpa"], pa_w8[(j - 2, 3)] + 1)
                    last = None
                    for qr in range(4):
                        w_sl = wq_in[:, qr * 1024:(qr + 1) * 1024]
                        vector.tensor_scalar(out=u0b[:], in0=w_sl,
                                             scalar1=st_t[:, 10:11], scalar2=128.5,
                                             op0=ALU.mult, op1=ALU.add)
                        vector.tensor_copy(out=ub[:], in_=u0b[:])
                        for t_ in LADDER:
                            vector.tensor_scalar(out=dbu[:], in0=ub[:], scalar1=t_,
                                                 scalar2=-t_, op0=ALU.is_ge, op1=ALU.mult)
                            vector.tensor_add(ub[:], ub[:], dbu[:])
                        vector.scalar_tensor_tensor(out=dbu[:], in0=ub[:], scalar=-1.0,
                                                    in1=u0b[:], op0=ALU.mult, op1=ALU.add)
                        last = vector.tensor_scalar(
                            out=w8c[:, j % 2, qr * 1024:(qr + 1) * 1024],
                            in0=dbu[:], scalar1=-128.0, scalar2=None, op0=ALU.add)
                    last.then_inc(sem["vbq"], 1)

                for h in range(2):
                    for g in range(NG):
                        vector.wait_ge(sem["tbw"], h * 128 + (g + 1) * CCH)
                        for f in range(FG):
                            fc = g * FG + f
                            vector.tensor_scalar(out=fstg[:], in0=pB[f][:],
                                                 scalar1=bq_cols[:, fc:fc + 1],
                                                 scalar2=None, op0=ALU.add)
                            if fc < 8:
                                if h == 1:
                                    vector.wait_ge(sem["SD"], sd_q_done[(0, fc)])
                                dst_h, dst_l = qstage[:, fc, 0], qstage[:, fc, 1]
                            elif fc < 16:
                                dc = fc - 8
                                if h == 1:
                                    vector.wait_ge(sem["GD"], gd_k_done[(0, dc)])
                                dst_h, dst_l = kstage[:, dc, 0], kstage[:, dc, 1]
                            else:
                                dc = fc - 16
                                if h == 1:
                                    vector.wait_ge(sem["tpa"], pa_vt[(0, dc, 1)] + 1)
                                dst_h, dst_l = vsplit[:, dc, 0], vsplit[:, dc, 1]
                            vector.tensor_copy(out=dst_h, in_=fstg[:])
                            vector.tensor_tensor(out=dst_l, in0=fstg[:], in1=dst_h,
                                                 op=ALU.subtract).then_inc(sem["vbe"], 1)
                        if g == 0:
                            wquant(2 * h)
                        if g == 1:
                            wquant(2 * h + 1)
                        for u_ in range(8):
                            pc = g * 8 + u_
                            gidx = h * 32 + pc
                            if gidx >= 2:
                                ph, pp = divmod(gidx - 2, 32)
                                vector.wait_ge(sem["SD"], sd_x8_done[(ph, pp)])
                            cc = pc
                            vector.tensor_tensor(out=ub[:, 0:512], in0=xnfh[:, cc],
                                                 in1=xnfl[:, cc], op=ALU.add)
                            vector.tensor_tensor(out=u0b[:, 0:512], in0=ub[:, 0:512],
                                                 in1=m_b[:, h * 512:(h + 1) * 512],
                                                 op=ALU.mult)
                            vector.tensor_scalar(out=u0b[:, 0:512], in0=u0b[:, 0:512],
                                                 scalar1=128.5, scalar2=None, op0=ALU.add)
                            vector.tensor_copy(out=ub[:, 0:512], in_=u0b[:, 0:512])
                            for t_ in LADDER:
                                vector.tensor_scalar(out=dbu[:, 0:512], in0=ub[:, 0:512],
                                                     scalar1=t_, scalar2=-t_,
                                                     op0=ALU.is_ge, op1=ALU.mult)
                                vector.tensor_add(ub[:, 0:512], ub[:, 0:512],
                                                  dbu[:, 0:512])
                            vector.scalar_tensor_tensor(out=dbu[:, 0:512],
                                                        in0=ub[:, 0:512], scalar=-1.0,
                                                        in1=u0b[:, 0:512],
                                                        op0=ALU.mult, op1=ALU.add)
                            vector.tensor_scalar(out=x8st[:, pc % 2], in0=dbu[:, 0:512],
                                                 scalar1=-128.0, scalar2=None,
                                                 op0=ALU.add).then_inc(sem["vbx"], 1)
                    for jj in range(2):
                        j = 2 * h + jj
                        for grp in range(4):
                            kk = pa_w8[(j, grp)]
                            extra = None
                            prevs = [key for key, v_ in pa_w8.items() if v_ == kk - 2]
                            if prevs:
                                def extra(vector=vector, key=prevs[0]):
                                    vector.wait_ge(sem["GD"], gd_w8_done[key])
                            kk2 = pa_evac(vector, w8ts[:, kk % 2], extra_wait=extra)
                            assert kk2 == kk
                    for dc in range(DCH):
                        for hl in range(2):
                            kk = pa_vt[(h, dc, hl)]
                            extra = None
                            prevs = [key for key, v_ in pa_vt.items() if v_ == kk - 2]
                            if prevs:
                                def extra(vector=vector, key=prevs[0]):
                                    vector.wait_ge(sem["GD"], gd_v_done[key])
                            kk2 = pa_evac(vector, vtst[:, kk % 2], extra_wait=extra,
                                          width=512)
                            assert kk2 == kk

        assert pa_t["n"] == PA_B and pa_v["n"] == PA_B

        dctx = ExitStack()
        with dctx:
            qht = dctx.enter_context(nc.sbuf_tensor("D_qh", [128, 2, DCH, 128], BF16))
            qlt = dctx.enter_context(nc.sbuf_tensor("D_ql", [128, 2, DCH, 128], BF16))
            kbuf = dctx.enter_context(nc.sbuf_tensor("D_kb", [128, 4, 2, DCH, 512], BF16))
            Sb = dctx.enter_context(nc.sbuf_tensor("D_S", [128, S], F32))
            Eb = dctx.enter_context(nc.sbuf_tensor("D_E", [128, S], F32))
            ehl = dctx.enter_context(nc.sbuf_tensor("D_ehl", [128, 2, 2, 512], BF16))
            etb = dctx.enter_context(nc.sbuf_tensor("D_et", [128, 2, 8, 128], BF16))
            vbuf = dctx.enter_context(nc.sbuf_tensor("D_vb", [128, 4, 2, 1024], BF16))
            osp = dctx.enter_context(nc.sbuf_tensor("D_os", [128, 1024], F32))
            osplit = dctx.enter_context(nc.sbuf_tensor("D_ox", [128, 2, 1024], BF16))
            otst = dctx.enter_context(nc.sbuf_tensor("D_ot", [128, 2, 1024], BF16))
            rst = dctx.enter_context(nc.sbuf_tensor("D_r", [128, 8], F32))

            pa_et = {}
            pa_ot = {}
            k0 = PA_B
            for qt in range(NTT):
                for sl in range(16):
                    pa_et[(qt, sl)] = k0
                    k0 += 1
                for hl in range(2):
                    pa_ot[(qt, hl)] = k0
                    k0 += 1
            PA_D = k0

            sd_v = SD_B
            sd_klo_done = {}
            sd_osp_done = {}
            for qt in range(NTT):
                for kb in range(16):
                    sd_v += 8 * 16
                    sd_klo_done[(qt, kb)] = sd_v
                sd_v += 2 * 16
                sd_osp_done[qt] = sd_v
            SD_D = sd_v

            ld_v = LD_B
            ld_q_done = {}
            ld_khi_done = {}
            for qt in range(NTT):
                ld_v += 16 * 16
                ld_q_done[qt] = ld_v
                for kb in range(16):
                    ld_v += 8 * 16
                    ld_khi_done[(qt, kb)] = ld_v
            LD_D = ld_v

            gd_v = GD_B
            gd_vb_done = {}
            for qt in range(NTT):
                for kt in range(64):
                    gd_v += 32
                    gd_vb_done[(qt, kt)] = gd_v
            GD_D = gd_v

            @block.sync
            def _(sync):
                sync.wait_ge(sem["CC"], 10)
                sync.wait_ge(sem["SD"], SD_B)
                for qt in range(NTT):
                    if qt >= 2:
                        sync.wait_ge(sem["tds"], (qt - 1) * 16)
                    for dc in range(DCH):
                        sync.dma_start(out=qht[:, qt % 2, dc],
                                       in_=sp_qh[dc, :, qt * 128:(qt + 1) * 128]
                                       ).then_inc(sem["LD"], 16)
                        sync.dma_start(out=qlt[:, qt % 2, dc],
                                       in_=sp_ql[dc, :, qt * 128:(qt + 1) * 128]
                                       ).then_inc(sem["LD"], 16)
                    for kb in range(16):
                        gkb = qt * 16 + kb
                        if gkb >= 4:
                            sync.wait_ge(sem["tds"], gkb - 3)
                        r_, off = divmod(kb * 512, T)
                        for dc in range(DCH):
                            sync.dma_start(out=kbuf[:, gkb % 4, 0, dc],
                                           in_=ago_kh[r_, dc, :, off:off + 512]
                                           ).then_inc(sem["LD"], 16)

            @block.scalar
            def _(scalar):
                scalar.wait_ge(sem["CC"], 10)
                for qt in range(NTT):
                    for kb in range(16):
                        gkb = qt * 16 + kb
                        if gkb >= 4:
                            scalar.wait_ge(sem["tds"], gkb - 3)
                        r_, off = divmod(kb * 512, T)
                        for dc in range(DCH):
                            scalar.dma_start(out=kbuf[:, gkb % 4, 1, dc],
                                             in_=ago_kl[r_, dc, :, off:off + 512]
                                             ).then_inc(sem["SD"], 16)
                    scalar.wait_ge(sem["vdm"], qt + 1)
                    scalar.activation(out=Eb[:], in_=Sb[:], func=ACT.Exp,
                                      scale=INV32, bias=rst[:, 1:2]
                                      ).then_inc(sem["ssx"], 1)
                    scalar.wait_ge(sem["vpa"], pa_ot[(qt, 1)] + 1)
                    scalar.dma_start(
                        out=sp_oh[:, :, qt * 128:(qt + 1) * 128].transpose([1, 0, 2]),
                        in_=otst[:, pa_ot[(qt, 0)] % 2]
                           .rearrange("p (dc q) -> p dc q", dc=8),
                    ).then_inc(sem["SD"], 16)
                    scalar.dma_start(
                        out=sp_ol[:, :, qt * 128:(qt + 1) * 128].transpose([1, 0, 2]),
                        in_=otst[:, pa_ot[(qt, 1)] % 2]
                           .rearrange("p (dc q) -> p dc q", dc=8),
                    ).then_inc(sem["SD"], 16)

            @block.gpsimd
            def _(gpsimd):
                gpsimd.wait_ge(sem["CC"], 10)
                for qt in range(NTT):
                    for kt in range(64):
                        gkt = qt * 64 + kt
                        if gkt >= 4:
                            gpsimd.wait_ge(sem["tdp"], gkt - 3)
                        r_, rem = divmod(kt * 128, T)
                        tch = rem // 128
                        gpsimd.dma_start(out=vbuf[:, gkt % 4, 0],
                                         in_=ago_vh[r_, tch, :, :]).then_inc(sem["GD"], 16)
                        gpsimd.dma_start(out=vbuf[:, gkt % 4, 1],
                                         in_=ago_vl[r_, tch, :, :]).then_inc(sem["GD"], 16)

            @block.tensor
            def _(tensor):
                for qt in range(NTT):
                    for kb in range(16):
                        gkb = qt * 16 + kb
                        tensor.wait_ge(sem["LD"], ld_khi_done[(qt, kb)])
                        tensor.wait_ge(sem["SD"], sd_klo_done[(qt, kb)])
                        if gkb >= 2:
                            tensor.wait_ge(sem["vds"], gkb - 1)
                        bank = pB[gkb % 2]
                        last = None
                        for dc in range(DCH):
                            qh_ = qht[:, qt % 2, dc]
                            ql_ = qlt[:, qt % 2, dc]
                            kh_ = kbuf[:, gkb % 4, 0, dc]
                            kl_ = kbuf[:, gkb % 4, 1, dc]
                            tensor.matmul(bank[:], qh_, kh_, start=(dc == 0), stop=False)
                            tensor.matmul(bank[:], qh_, kl_, start=False, stop=False)
                            last = tensor.matmul(bank[:], ql_, kh_, start=False,
                                                 stop=(dc == DCH - 1))
                        last.then_inc(sem["tds"], 1)
                    for sl in range(16):
                        tensor.wait_ge(sem["vde"], qt * 16 + sl + 1)
                        pieces = []
                        for su in range(4):
                            pieces.append(ehl[:, sl % 2, 0, su * 128:(su + 1) * 128])
                            pieces.append(ehl[:, sl % 2, 1, su * 128:(su + 1) * 128])
                        kk = pa_group(tensor, pieces)
                        assert kk == pa_et[(qt, sl)]
                        tensor.wait_ge(sem["vpa"], kk + 1)
                        for su in range(4):
                            kt = sl * 4 + su
                            tensor.wait_ge(sem["GD"], gd_vb_done[(qt, kt)])
                            eh_ = etb[:, sl % 2, su * 2]
                            el_ = etb[:, sl % 2, su * 2 + 1]
                            last = None
                            for dh in range(2):
                                vh_ = vbuf[:, (qt * 64 + kt) % 4, 0, dh * 512:(dh + 1) * 512]
                                vl_ = vbuf[:, (qt * 64 + kt) % 4, 1, dh * 512:(dh + 1) * 512]
                                ob_ = pB[2 + dh]
                                tensor.matmul(ob_[:], eh_, vh_, start=(kt == 0), stop=False)
                                tensor.matmul(ob_[:], eh_, vl_, start=False, stop=False)
                                last = tensor.matmul(ob_[:], el_, vh_, start=False,
                                                     stop=(kt == 63))
                            last.then_inc(sem["tdp"], 1)
                    tensor.wait_ge(sem["vdo"], qt + 1)
                    for hl in range(2):
                        pieces = [osplit[:, hl, u_ * 128:(u_ + 1) * 128]
                                  for u_ in range(8)]
                        kk = pa_group(tensor, pieces)
                        assert kk == pa_ot[(qt, hl)]

            @block.vector
            def _(vector):
                vector.memset(rst[:], 0.0)
                for qt in range(NTT):
                    for kb in range(16):
                        gkb = qt * 16 + kb
                        vector.wait_ge(sem["tds"], gkb + 1)
                        vector.tensor_copy(out=Sb[:, kb * 512:(kb + 1) * 512],
                                           in_=pB[gkb % 2][:])
                        vector.tensor_copy(out=st_t[:, 12:13], in_=st_t[:, 13:14]
                                           ).then_inc(sem["vds"], 1)
                    vector.tensor_reduce(out=rst[:, 0:1], in_=Sb[:], axis=AX,
                                         op=ALU.max, negate=True)
                    vector.tensor_copy(out=rst[:, 4:5], in_=rst[:, 5:6])
                    vector.tensor_scalar(out=rst[:, 1:2], in0=rst[:, 0:1],
                                         scalar1=INV32, scalar2=None, op0=ALU.mult)
                    vector.tensor_copy(out=rst[:, 4:5], in_=rst[:, 5:6]
                                       ).then_inc(sem["vdm"], 1)
                    vector.wait_ge(sem["ssx"], qt + 1)
                    vector.tensor_reduce(out=rst[:, 2:3], in_=Eb[:], axis=AX, op=ALU.add)
                    vector.tensor_copy(out=rst[:, 4:5], in_=rst[:, 5:6])
                    vector.reciprocal(out=rst[:, 3:4], in_=rst[:, 2:3])
                    vector.tensor_copy(out=rst[:, 4:5], in_=rst[:, 5:6])
                    for sl in range(16):
                        gsl = qt * 16 + sl
                        if gsl >= 2:
                            vector.wait_ge(sem["tpa"],
                                           pa_et[divmod(gsl - 2, 16)] + 1)
                        esl = Eb[:, sl * 512:(sl + 1) * 512]
                        vector.tensor_copy(out=ehl[:, sl % 2, 0], in_=esl)
                        vector.tensor_tensor(out=ehl[:, sl % 2, 1], in0=esl,
                                             in1=ehl[:, sl % 2, 0],
                                             op=ALU.subtract).then_inc(sem["vde"], 1)
                        kk = pa_et[(qt, sl)]
                        extra = None
                        if gsl >= 2:
                            pq, psl = divmod(gsl - 2, 16)
                            def extra(vector=vector, pq=pq, psl=psl):
                                vector.wait_ge(sem["tdp"], pq * 64 + psl * 4 + 4)
                        kk2 = pa_evac(vector,
                                      etb[:, sl % 2].rearrange("p a b -> p (a b)"),
                                      extra_wait=extra)
                        assert kk2 == kk
                    vector.wait_ge(sem["tdp"], (qt + 1) * 64)
                    vector.tensor_scalar(out=osp[:, 0:512], in0=pB[2][:],
                                         scalar1=rst[:, 3:4], scalar2=None, op0=ALU.mult)
                    vector.tensor_scalar(out=osp[:, 512:1024], in0=pB[3][:],
                                         scalar1=rst[:, 3:4], scalar2=None, op0=ALU.mult)
                    vector.tensor_copy(out=osplit[:, 0], in_=osp[:])
                    vector.tensor_tensor(out=osplit[:, 1], in0=osp[:], in1=osplit[:, 0],
                                         op=ALU.subtract).then_inc(sem["vdo"], 1)
                    for hl in range(2):
                        kk = pa_ot[(qt, hl)]
                        extra = None
                        if qt >= 1:
                            def extra(vector=vector, qt=qt):
                                vector.wait_ge(sem["SD"], sd_osp_done[qt - 1])
                        kk2 = pa_evac(vector, otst[:, kk % 2], extra_wait=extra)
                        assert kk2 == kk

        assert pa_t["n"] == PA_D and pa_v["n"] == PA_D

        ectx = ExitStack()
        with ectx:
            x8f = ectx.enter_context(nc.sbuf_tensor("E_x8", [128, CCH, T], BF16))
            ohE = ectx.enter_context(nc.sbuf_tensor("E_oh", [128, DCH, T], BF16))
            olE = ectx.enter_context(nc.sbuf_tensor("E_ol", [128, DCH, T], BF16))
            w8b = ectx.enter_context(nc.sbuf_tensor("E_w8", [128, 2, CCH, EN], BF16))
            wohb = ectx.enter_context(nc.sbuf_tensor("E_wh", [128, 2, DCH, EN], BF16))
            wolb = ectx.enter_context(nc.sbuf_tensor("E_wl", [128, 2, DCH, EN], BF16))
            bo_b = ectx.enter_context(nc.sbuf_tensor("E_bo", [128, 2, EN], F32))
            ws_b = ectx.enter_context(nc.sbuf_tensor("E_ws", [128, 2, EN], F32))
            x_t = ectx.enter_context(nc.sbuf_tensor("E_x", [128, 2, EN], F32))
            sstg = ectx.enter_context(nc.sbuf_tensor("E_s", [128, EN], F32))
            gstg = ectx.enter_context(nc.sbuf_tensor("E_g", [128, EN], F32))
            gsil = ectx.enter_context(nc.sbuf_tensor("E_gs", [128, EN], F32))
            ostg = ectx.enter_context(nc.sbuf_tensor("E_o", [128, 2, EN], F32))

            ld_v = LD_D + (32 + 16) * 16
            LD_E_res = ld_v
            ld_wchunk_done = {}
            for j in range(ENC):
                ld_v += (32 + 16) * 16
                ld_wchunk_done[j] = ld_v
            LD_E = ld_v

            gd_v = GD_D
            gd_bws_done = {}
            gd_x_done = {}
            gd_store_done = {}
            for j in range(ENC):
                gd_v += 32
                gd_bws_done[j] = gd_v
                for tt in range(NTT):
                    gd_v += 16
                    gd_x_done[(j, tt)] = gd_v
                    gd_v += 16
                    gd_store_done[(j, tt)] = gd_v
            GD_E = gd_v

            @block.sync
            def _(sync):
                sync.wait_ge(sem["CC"], 10)
                sync.wait_ge(sem["SD"], SD_D)
                for cc in range(CCH):
                    sync.dma_start(out=x8f[:, cc], in_=sp_x8[cc]).then_inc(sem["LD"], 16)
                for dc in range(DCH):
                    sync.dma_start(out=ohE[:, dc], in_=sp_oh[dc]).then_inc(sem["LD"], 16)
                    sync.dma_start(out=olE[:, dc], in_=sp_ol[dc]).then_inc(sem["LD"], 16)
                for j in range(ENC):
                    if j >= 2:
                        sync.wait_ge(sem["tef"], (j - 1) * NTT)
                    r_, half_c = divmod(j, 2)
                    for cc in range(CCH):
                        sync.dma_start(
                            out=w8b[:, j % 2, cc],
                            in_=ago_w8t[r_, cc, :, half_c * EN:(half_c + 1) * EN]
                        ).then_inc(sem["LD"], 16)
                    for dc in range(DCH):
                        sync.dma_start(
                            out=wohb[:, j % 2, dc],
                            in_=ago_woh[dc * 128:(dc + 1) * 128, j * EN:(j + 1) * EN]
                        ).then_inc(sem["LD"], 16)
                        sync.dma_start(
                            out=wolb[:, j % 2, dc],
                            in_=ago_wol[dc * 128:(dc + 1) * 128, j * EN:(j + 1) * EN]
                        ).then_inc(sem["LD"], 16)

            @block.gpsimd
            def _(gpsimd):
                gpsimd.wait_ge(sem["CC"], 10)
                for j in range(ENC):
                    if j >= 2:
                        gpsimd.wait_ge(sem["vee"], (j - 1) * NTT)
                    gpsimd.dma_start(out=bo_b[:, j % 2],
                                     in_=bo_d[j * EN:(j + 1) * EN].unsqueeze(0)
                                     .broadcast_to([128, EN])).then_inc(sem["GD"], 16)
                    gpsimd.dma_start(out=ws_b[:, j % 2],
                                     in_=ago_wsc[j * EN:(j + 1) * EN].unsqueeze(0)
                                     .broadcast_to([128, EN])).then_inc(sem["GD"], 16)
                    for tt in range(NTT):
                        it = j * NTT + tt
                        if it >= 2:
                            gpsimd.wait_ge(sem["vee"], it - 1)
                        gpsimd.dma_start(
                            out=x_t[:, it % 2],
                            in_=x_d[tt * 128:(tt + 1) * 128, j * EN:(j + 1) * EN]
                        ).then_inc(sem["GD"], 16)
                        gpsimd.wait_ge(sem["vee"], it + 1)
                        gpsimd.dma_start(
                            out=out_d[tt * 128:(tt + 1) * 128, j * EN:(j + 1) * EN],
                            in_=ostg[:, it % 2]).then_inc(sem["GD"], 16)
                gpsimd.wait_ge(sem["GD"], GD_E)

            @block.scalar
            def _(scalar):
                for j in range(ENC):
                    for tt in range(NTT):
                        it = j * NTT + tt
                        scalar.wait_ge(sem["vgs"], it + 1)
                        scalar.activation(out=gsil[:], in_=gstg[:], func=ACT.Silu
                                          ).then_inc(sem["ssl"], 1)

            @block.tensor
            def _(tensor):
                tensor.wait_ge(sem["LD"], LD_E_res)
                for j in range(ENC):
                    tensor.wait_ge(sem["LD"], ld_wchunk_done[j])
                    for tt in range(NTT):
                        it = j * NTT + tt
                        if it >= 2:
                            tensor.wait_ge(sem["vee"], it - 1)
                        sb = pB[it % 2]
                        fb = pB[2 + it % 2]
                        for dc in range(DCH):
                            oh_ = ohE[:, dc, tt * 128:(tt + 1) * 128]
                            ol_ = olE[:, dc, tt * 128:(tt + 1) * 128]
                            wh_ = wohb[:, j % 2, dc]
                            wl_ = wolb[:, j % 2, dc]
                            tensor.matmul(sb[:, 0:EN], oh_, wh_, start=(dc == 0),
                                          stop=False)
                            tensor.matmul(sb[:, 0:EN], oh_, wl_, start=False, stop=False)
                            tensor.matmul(sb[:, 0:EN], ol_, wh_, start=False,
                                          stop=(dc == DCH - 1))
                        last = None
                        for cc in range(CCH):
                            x8_ = x8f[:, cc, tt * 128:(tt + 1) * 128]
                            w8_ = w8b[:, j % 2, cc]
                            last = tensor.matmul(fb[:, 0:EN], x8_, w8_, start=(cc == 0),
                                                 stop=(cc == CCH - 1))
                        last.then_inc(sem["tef"], 1)

            @block.vector
            def _(vector):
                for j in range(ENC):
                    for tt in range(NTT):
                        it = j * NTT + tt
                        vector.wait_ge(sem["tef"], it + 1)
                        vector.wait_ge(sem["GD"], gd_x_done[(j, tt)])
                        if it >= 2:
                            pj, ptt = divmod(it - 2, NTT)
                            vector.wait_ge(sem["GD"], gd_store_done[(pj, ptt)])
                        sb = pB[it % 2]
                        fb = pB[2 + it % 2]
                        vector.scalar_tensor_tensor(out=sstg[:], in0=sb[:, 0:EN],
                                                    scalar=1.0, in1=bo_b[:, j % 2],
                                                    op0=ALU.bypass, op1=ALU.add)
                        vector.scalar_tensor_tensor(out=gstg[:], in0=fb[:, 0:EN],
                                                    scalar=xs_all[:, tt:tt + 1],
                                                    in1=ws_b[:, j % 2],
                                                    op0=ALU.mult, op1=ALU.mult
                                                    ).then_inc(sem["vgs"], 1)
                        vector.wait_ge(sem["ssl"], it + 1)
                        vector.tensor_tensor(out=gsil[:], in0=sstg[:], in1=gsil[:],
                                             op=ALU.mult)
                        vector.tensor_tensor(out=ostg[:, it % 2], in0=gsil[:],
                                             in1=x_t[:, it % 2],
                                             op=ALU.add).then_inc(sem["vee"], 1)

    return nc


_NC_CACHE = None


def kernel(L0_attn_x, L0_attn_Wqkv, L0_attn_bqkv, L0_attn_Wo, L0_attn_bo,
           L0_ffn_weight):
    global _NC_CACHE
    x = np.ascontiguousarray(np.asarray(L0_attn_x, dtype=np.float32).reshape(S, C))
    wqkv = np.ascontiguousarray(np.asarray(L0_attn_Wqkv, dtype=np.float32))
    wo = np.ascontiguousarray(np.asarray(L0_attn_Wo, dtype=np.float32))
    w = np.ascontiguousarray(np.asarray(L0_ffn_weight, dtype=np.float32))
    bq = np.ascontiguousarray(np.asarray(L0_attn_bqkv, dtype=np.float32))
    bo = np.ascontiguousarray(np.asarray(L0_attn_bo, dtype=np.float32))
    eye = np.eye(128, dtype=np.float32)
    import ml_dtypes
    eyeb = np.eye(128, dtype=ml_dtypes.bfloat16)

    if _NC_CACHE is None:
        _NC_CACHE = build()
    nc = _NC_CACHE

    in_maps = []
    for i in range(R):
        in_maps.append({
            "x": x[i * T:(i + 1) * T],
            "wqkv_s": wqkv[i * 512:(i + 1) * 512],
            "wo_s": wo[i * 128:(i + 1) * 128],
            "w_s": w[i * 512:(i + 1) * 512],
            "bqkv": bq,
            "bo": bo,
            "eye": eye,
            "eyeb": eyeb,
        })
    res = run_bass_kernel_spmd(nc, in_maps, core_ids=list(range(R)))
    out = np.concatenate([res.results[i]["out"] for i in range(R)], axis=0)
    return out.reshape(1, S, C)
